# version 06
# speedup vs baseline: 1.3256x; 1.0104x over previous
import numpy as np
import ml_dtypes
import concourse.bass as bass
import concourse.mybir as mybir
from concourse.bass_utils import run_bass_kernel_spmd

F32 = mybir.dt.float32
BF16 = mybir.dt.bfloat16
AF = mybir.ActivationFunctionType
ALU = mybir.AluOpType
AXX = mybir.AxisListType.X
BF = ml_dtypes.bfloat16

D = 2048
T = 2048
HID = 8192
NDC = D // 128
NTT = T // 128
NHC = HID // 128
NH = 16
NKV = 4
C32 = 12582912.0
C64 = 6755399441055744.0
LN_EPS = 1e-5
RMS_EPS = 1e-6
Q_EPS = 1e-5

DEV_STOP = None
DEV_DUMPS = []


class Sem:

    def __init__(self, nc, name):
        self.h = nc.alloc_semaphore(name)
        self.n = 0

    def inc(self, inst, k=1):
        inst.then_inc(self.h, k)
        self.n += k
        return self.n

    def dinc(self, inst):
        return self.inc(inst, 16)

    def chain(self, eng, inst, k=1):
        self.inc(inst, k)
        eng.wait_ge(self.h, self.n)


class KB:

    def __init__(self):
        nc = bass.Bass("TRN2", target_bir_lowering=False,
                       detect_race_conditions=False)
        self.nc = nc

        def inp(name, shape, dtype=F32):
            return nc.dram_tensor(name, shape, dtype, kind="ExternalInput").ap()

        def scratch(name, shape, dtype=F32):
            return nc.dram_tensor(name, shape, dtype, kind="Internal").ap()

        self.x0 = inp("x0", [T, D])
        self.cwT = inp("cwT", [NDC, 128, NDC, 128])
        self.dwT = inp("dwT", [NDC, 128, NDC, 128])
        self.cb = inp("cb", [128, NDC])
        self.db = inp("db", [128, NDC])
        self.ldl = inp("ldl", [128, NDC])
        self.abv = inp("abv", [128, NDC * 64])
        self.bbv = inp("bbv", [128, NDC * 64])
        self.wg0 = inp("wg0", [NHC, 128, NDC, 128])
        self.wu0 = inp("wu0", [NHC, 128, NDC, 128])
        self.wd0 = inp("wd0", [NHC, 128, NDC, 128])
        self.wq = inp("wq", [NH, 128, NDC, 128])
        self.wk = inp("wk", [NKV, 128, NDC, 128])
        self.wv = inp("wv", [NKV, 128, NDC, 128])
        self.wo = inp("wo", [NDC, 128, NH, 128])
        self.gq = inp("gq", [128, 1])
        self.gk = inp("gk", [128, 1])
        self.wg1 = inp("wg1", [NHC, 128, NDC, 128])
        self.wu1 = inp("wu1", [NHC, 128, NDC, 128])
        self.wd1 = inp("wd1", [NHC, 128, NDC, 128])
        self.identb = inp("identb", [128, 128], BF16)
        self.identf = inp("identf", [128, 128])
        self.cmask = inp("cmask", [128, 128])
        self.h128 = inp("h128", [128, 128], BF16)
        self.h64b = inp("h64b", [128, 128], BF16)
        self.onesb = inp("onesb", [128, 1], BF16)
        self.onesf = inp("onesf", [128, 1])
        self.onesr = inp("onesr", [1, 128])

        self.attn0s = scratch("attn0s", [NDC, 128, T])
        self.hid0s = scratch("hid0s", [NHC, 128, T], BF16)
        self.x1s = scratch("x1s", [T, D])
        self.attn1s = scratch("attn1s", [NH, 128, T], BF16)
        self.woS = scratch("woS", [NDC, 128, T])
        self.hid1s = scratch("hid1s", [NHC, 128, T], BF16)
        self.xqs = scratch("xqs", [NDC, 128, T], BF16)
        self.s1s = scratch("s1s", [HID, T], BF16)
        self.wg1q = scratch("wg1q", [NHC, 128, NDC, 128], BF16)
        self.wu1q = scratch("wu1q", [NHC, 128, NDC, 128], BF16)
        self.wd1q = scratch("wd1q", [NHC, 128, NDC, 128], BF16)

        self.out = nc.dram_tensor("out", [T, D], F32, kind="ExternalOutput").ap()

        def sb(name, shape, dtype=F32):
            return nc.alloc_sbuf_tensor(name, shape, dtype).ap()

        self.c_identb = sb("c_identb", [128, 128], BF16)
        self.c_identf = sb("c_identf", [128, 128])
        self.c_cmask = sb("c_cmask", [128, 128])
        self.c_h128 = sb("c_h128", [128, 128], BF16)
        self.c_h64b = sb("c_h64b", [128, 128], BF16)
        self.c_onesb = sb("c_onesb", [128, 1], BF16)
        self.c_onesf = sb("c_onesf", [128, 1])
        self.c_onesr = sb("c_onesr", [1, 128])
        self.c_gq = sb("c_gq", [128, 1])
        self.c_gk = sb("c_gk", [128, 1])
        self.kk = sb("kk", [128, 1])
        self.cb_t = sb("cb_t", [128, NDC])
        self.db_t = sb("db_t", [128, NDC])
        self.adiag = sb("adiag", [128, NDC])
        self.abprod = sb("abprod", [128, NDC])
        self.mean0 = sb("mean0", [128, NTT])
        self.rstd0 = sb("rstd0", [128, NTT])
        self.mean1 = sb("mean1", [128, NTT])
        self.rstd1 = sb("rstd1", [128, NTT])
        self.sact1 = sb("sact1", [128, NTT])
        self.rs1 = sb("rs1", [128, NTT])
        self.swg = sb("swg", [128, 1])
        self.swu = sb("swu", [128, 1])
        self.swd = sb("swd", [128, 1])
        self.iswg = sb("iswg", [128, 1])
        self.iswu = sb("iswu", [128, 1])
        self.iswd = sb("iswd", [128, 1])
        self.tiny = sb("tiny", [128, 64])
        self.c_epsr = sb("c_epsr", [128, 1])
        self.c_sqrt2 = sb("c_sqrt2", [128, 1])

        self.xnf = sb("xnf", [128, NDC * T], BF16)
        self.xnf3 = self.xnf.rearrange("p (c t) -> p c t", c=NDC)

        self.sld = Sem(nc, "sld")
        self.sst = Sem(nc, "sst")
        self.sv = Sem(nc, "sv")
        self.ss = Sem(nc, "ss")
        self.st = Sem(nc, "st")
        self.sb_ = Sem(nc, "sb")
        self.sout = Sem(nc, "sout")

    def xn_chunk(self, dc, c0=0, c1=T):
        return self.xnf[:, dc * T + c0: dc * T + c1]

    def xq_chunk(self, dc, c0=0, c1=T):
        return self.xqf[:, dc * T + c0: dc * T + c1]


def emit_round(vector, out_ap, y_ap, t1, t2, kk):
    vector.tensor_scalar(out=t1, in0=y_ap, scalar1=C32, scalar2=-C32,
                         op0=ALU.add, op1=ALU.add)
    vector.tensor_scalar(out=t2, in0=y_ap, scalar1=C64, scalar2=-C64,
                         op0=ALU.add, op1=ALU.add)
    return vector.scalar_tensor_tensor(out=out_ap, in0=t1, scalar=kk,
                                       in1=t2, op0=ALU.mult, op1=ALU.add)


def phase_init(K):
    nc = K.nc
    ld, sv, ss = K.sld, K.sv, K.ss
    with (
        nc.sbuf_tensor("abstage", [128, 3 * NDC * 64], F32) as abstage,
        nc.Block() as blk,
    ):
        @blk.sync
        def _(sync):
            for dst, src in [
                (K.c_identb, K.identb), (K.c_identf, K.identf),
                (K.c_cmask, K.cmask), (K.c_h128, K.h128), (K.c_h64b, K.h64b),
                (K.c_onesb, K.onesb),
                (K.c_onesf, K.onesf),
                (K.c_onesr, K.onesr),
                (K.c_gq, K.gq), (K.c_gk, K.gk),
                (K.cb_t, K.cb), (K.db_t, K.db),
                (K.adiag, K.ldl),
                (abstage[:, 0:1024], K.abv),
                (abstage[:, 1024:2048], K.bbv),
            ]:
                ld.dinc(sync.dma_start(out=dst, in_=src))

        @blk.vector
        def _(vector):
            vector.wait_ge(ld.h, ld.n)
            sv.chain(vector, vector.memset(K.kk, 1.0))
            sv.chain(vector, vector.memset(K.c_epsr, RMS_EPS))
            sv.chain(vector, vector.memset(K.c_sqrt2, 2.0))
            sv.chain(vector, vector.tensor_scalar(
                out=K.tiny[:, 0:1], in0=K.kk, scalar1=C64,
                scalar2=-C64, op0=ALU.add, op1=ALU.add))
            sv.chain(vector, vector.tensor_scalar(
                out=K.kk, in0=K.tiny[:, 0:1], scalar1=-1.0,
                scalar2=1.0, op0=ALU.mult, op1=ALU.add))
            sv.chain(vector, vector.tensor_mul(
                abstage[:, 2048:3072], abstage[:, 0:1024],
                abstage[:, 1024:2048]))
            for c in range(NDC):
                vector.reduce_sum(out=K.abprod[:, c:c + 1],
                                  in_=abstage[:, 2048 + c * 64: 2048 + (c + 1) * 64],
                                  axis=AXX)
            sv.inc(vector.tensor_copy(out=K.tiny[:, 1:2], in_=K.kk))
            vector.wait_ge(sv.h, sv.n)

        @blk.scalar
        def _(scalar):
            scalar.wait_ge(ld.h, ld.n)
            ss.inc(scalar.activation(out=K.adiag, in_=K.adiag,
                                     func=AF.Sigmoid, scale=-1.0))
            scalar.wait_ge(K.sv.h, K.sv.n)
            ss.inc(scalar.activation(out=K.c_sqrt2, in_=K.c_sqrt2, func=AF.Sqrt))
            scalar.wait_ge(ss.h, ss.n)


def emit_ln_tile(K, vector, scalar, stat, tb, xnb, mean_dst, rstd_dst, tt, sv, ss):
    raise NotImplementedError


def phase_p0(K):
    nc = K.nc
    ld, sv, ss, st = K.sld, K.sv, K.ss, K.st
    with (
        nc.sbuf_tensor("p0_tbuf", [128, 2, T], F32) as tbuf,
        nc.sbuf_tensor("p0_xn", [128, 2, T], BF16) as xnbuf,
        nc.sbuf_tensor("p0_stat", [128, 40], F32) as stat,
        nc.psum_tensor("p0_ptr0", [128, 512], F32) as ptr0,
        nc.psum_tensor("p0_ptr1", [128, 512], F32) as ptr1,
        nc.Block() as blk,
    ):
        ptrs = [ptr0, ptr1]
        ld0, ss0, st0 = ld.n, ss.n, st.n
        v_recip = []
        v_bias = []
        v_evac = {}

        def emit_evacs(vector, tt):
            for g in range(4):
                i = tt * 4 + g
                vector.wait_ge(st.h, st0 + i + 1)
                dst = K.xnf3[:, 4 * g:4 * g + 4, tt * 128:(tt + 1) * 128]
                sv.inc(vector.tensor_copy(
                    out=dst, in_=ptrs[i % 2].rearrange("p (a b) -> p a b", a=4)))
                v_evac[i] = sv.n

        @blk.sync
        def _(sync):
            for tt in range(NTT):
                if tt >= 2:
                    sync.wait_ge(ss.h, ss0 + 2 * (tt - 2) + 2)
                ld.dinc(sync.dma_start(out=tbuf[:, tt % 2, :],
                                       in_=K.x0[tt * 128:(tt + 1) * 128, :]))

        @blk.vector
        def _(vector):
            for tt in range(NTT):
                vector.wait_ge(ld.h, ld0 + 16 * (tt + 1))
                tb = tbuf[:, tt % 2, :]
                inst = None
                for q4 in range(4):
                    inst = vector.bn_stats(out=stat[:, 6 * q4:6 * q4 + 6],
                                           in_=tb[:, q4 * 512:(q4 + 1) * 512])
                sv.chain(vector, inst)
                sv.chain(vector, vector.bn_aggr(out=stat[:, 24:26], in_=stat[:, 0:24]))
                sv.chain(vector, vector.tensor_scalar(
                    out=stat[:, 26:27], in0=stat[:, 25:26],
                    scalar1=LN_EPS, scalar2=None, op0=ALU.add))
                sv.inc(vector.reciprocal(out=stat[:, 27:28], in_=stat[:, 26:27]))
                v_recip.append(sv.n)
                vector.wait_ge(ss.h, ss0 + 2 * tt + 1)
                sv.chain(vector, vector.tensor_mul(stat[:, 29:30], stat[:, 28:29], stat[:, 28:29]))
                sv.chain(vector, vector.tensor_mul(stat[:, 29:30], stat[:, 29:30], stat[:, 26:27]))
                sv.chain(vector, vector.tensor_scalar(
                    out=stat[:, 29:30], in0=stat[:, 29:30],
                    scalar1=-0.5, scalar2=1.5, op0=ALU.mult, op1=ALU.add))
                sv.chain(vector, vector.tensor_mul(stat[:, 30:31], stat[:, 28:29], stat[:, 29:30]))
                vector.tensor_copy(out=K.mean0[:, tt:tt + 1], in_=stat[:, 24:25])
                vector.tensor_copy(out=K.rstd0[:, tt:tt + 1], in_=stat[:, 30:31])
                sv.inc(vector.tensor_scalar(out=stat[:, 31:32], in0=stat[:, 24:25],
                                            scalar1=stat[:, 30:31], scalar2=-1.0,
                                            op0=ALU.mult, op1=ALU.mult))
                v_bias.append(sv.n)
                if tt >= 1:
                    emit_evacs(vector, tt - 1)
            emit_evacs(vector, NTT - 1)

        @blk.scalar
        def _(scalar):
            for tt in range(NTT):
                scalar.wait_ge(sv.h, v_recip[tt])
                ss.inc(scalar.activation(out=stat[:, 28:29], in_=stat[:, 27:28],
                                         func=AF.Sqrt))
                scalar.wait_ge(sv.h, v_bias[tt])
                if tt >= 2:
                    scalar.wait_ge(st.h, st0 + 4 * (tt - 2) + 4)
                ss.inc(scalar.activation(out=xnbuf[:, tt % 2, :], in_=tbuf[:, tt % 2, :],
                                         func=AF.Identity, bias=stat[:, 31:32],
                                         scale=stat[:, 30:31]))

        @blk.tensor
        def _(tensor):
            for tt in range(NTT):
                tensor.wait_ge(ss.h, ss0 + 2 * tt + 2)
                xnb = xnbuf[:, tt % 2, :]
                for g in range(4):
                    i = tt * 4 + g
                    pt = ptrs[i % 2]
                    if i >= 2:
                        tensor.wait_ge(sv.h, v_evac[i - 2])
                    inst = None
                    for j in range(4):
                        dc = g * 4 + j
                        inst = tensor.matmul(out=pt[:, j * 128:(j + 1) * 128],
                                             lhsT=xnb[:, dc * 128:(dc + 1) * 128],
                                             rhs=K.c_identb,
                                             start=(j == 0), stop=(j == 3))
                    st.inc(inst)


def phase_p1(K):
    nc = K.nc
    ld, sv, ss, st, sst = K.sld, K.sv, K.ss, K.st, K.sst
    with (
        nc.sbuf_tensor("p1_cw", [128, 2048], F32) as cwb_,
        nc.sbuf_tensor("p1_dw", [128, 2048], F32) as dwb_,
        nc.sbuf_tensor("p1_cwb", [128, 2, 2048], BF16) as cwbf,
        nc.sbuf_tensor("p1_dwb", [128, 2, 2048], BF16) as dwbf,
        nc.sbuf_tensor("p1_ones", [128, T], BF16) as ones_,
        nc.sbuf_tensor("p1_sig", [128, T], F32) as sigb,
        nc.sbuf_tensor("p1_abc", [128, T], F32) as abc,
        nc.sbuf_tensor("p1_out", [128, T], F32) as outb,
        nc.psum_tensor("p1_pc", [128, 2048], F32) as pC,
        nc.psum_tensor("p1_pd", [128, 2048], F32) as pD,
        nc.Block() as blk,
    ):
        cwb, dwb = cwb_[:, :], dwb_[:, :]
        sigb_h, abc_h, outb_h = sigb, abc, outb
        sigb, abc, outb = sigb[:, :], abc[:, :], outb[:, :]
        ld0 = ld.n
        ss0 = ss.n
        st0p = st.n
        sst0 = sst.n
        sv0 = sv.n
        s_cast = [ss0 + 2 * dc + 1 for dc in range(NDC)]
        s_sig_f = [ss0 + 2 * dc + 2 for dc in range(NDC)]
        t_d_f = [st0p + 2 * dc + 1 for dc in range(NDC)]
        t_c_f = [st0p + 2 * dc + 2 for dc in range(NDC)]
        d_spill_f = [sst0 + 16 * (dc + 1) for dc in range(NDC)]
        v_xp_f = [sv0 + 1 + 4 * dc + 2 for dc in range(NDC)]
        v_scan_f = [sv0 + 1 + 4 * dc + 3 for dc in range(NDC)]
        v_out_f = [sv0 + 1 + 4 * dc + 4 for dc in range(NDC)]
        s_sig = []
        t_d = []
        t_c = []
        v_xp = []
        v_scan = []
        v_out = []
        d_spill = []

        @blk.sync
        def _(sync):
            for dc in range(NDC):
                if dc >= 1:
                    sync.wait_ge(ss.h, s_cast[dc - 1])
                ld.dinc(sync.dma_start(out=cwb, in_=K.cwT[dc]))
                ld.dinc(sync.dma_start(out=dwb, in_=K.dwT[dc]))

        @blk.scalar
        def _(scalar):
            for dc in range(NDC):
                scalar.wait_ge(ld.h, ld0 + 32 * (dc + 1))
                scalar.copy(out=cwbf[:, dc % 2, :], in_=cwb)
                ss.inc(scalar.copy(out=dwbf[:, dc % 2, :], in_=dwb))
                assert ss.n == s_cast[dc]
                scalar.wait_ge(st.h, t_d_f[dc])
                if dc >= 1:
                    scalar.wait_ge(sv.h, v_scan_f[dc - 1])
                ss.inc(scalar.activation(out=sigb, in_=pD[:, :], func=AF.Sigmoid,
                                         bias=K.db_t[:, dc:dc + 1], scale=1.0))
                s_sig.append(ss.n)
                assert ss.n == s_sig_f[dc]

        @blk.tensor
        def _(tensor):
            for dc in range(NDC):
                tensor.wait_ge(ss.h, s_cast[dc])
                if dc >= 1:
                    tensor.wait_ge(ss.h, s_sig[dc - 1])
                for tc in range(4):
                    inst = None
                    for g in range(NDC):
                        inst = tensor.matmul(
                            out=pD[:, tc * 512:(tc + 1) * 512],
                            lhsT=dwbf[:, dc % 2, g * 128:(g + 1) * 128],
                            rhs=K.xn_chunk(g, tc * 512, (tc + 1) * 512),
                            start=(g == 0), stop=(g == NDC - 1))
                st.inc(inst)
                t_d.append(st.n)
                assert st.n == t_d_f[dc]
                if dc >= 1:
                    tensor.wait_ge(sv.h, v_xp_f[dc - 1])
                for tc in range(4):
                    inst = None
                    for g in range(NDC):
                        inst = tensor.matmul(
                            out=pC[:, tc * 512:(tc + 1) * 512],
                            lhsT=cwbf[:, dc % 2, g * 128:(g + 1) * 128],
                            rhs=K.xn_chunk(g, tc * 512, (tc + 1) * 512),
                            start=(g == 0), stop=(g == NDC - 1))
                st.inc(inst)
                t_c.append(st.n)
                assert st.n == t_c_f[dc]

        @blk.vector
        def _(vector):
            sv.chain(vector, vector.memset(ones_[:, :], 1.0))
            for dc in range(NDC):
                sv.inc(vector.tensor_scalar_mul(abc, ones_[:, :],
                                                K.adiag[:, dc:dc + 1]))
                vector.wait_ge(st.h, t_c[dc])
                vector.wait_ge(ss.h, s_sig[dc])
                sv.chain(vector, vector.scalar_tensor_tensor(
                    out=sigb, in0=pC[:, :], scalar=K.cb_t[:, dc:dc + 1], in1=sigb,
                    op0=ALU.add, op1=ALU.mult))
                v_xp.append(sv.n)
                if dc >= 1:
                    vector.wait_ge(sst.h, d_spill_f[dc - 1])
                sv.chain(vector, vector.tensor_tensor_scan(
                    out=outb, data0=abc, data1=sigb, initial=0.0,
                    op0=ALU.mult, op1=ALU.add))
                v_scan.append(sv.n)
                sv.inc(vector.tensor_scalar_mul(outb, outb,
                                                K.abprod[:, dc:dc + 1]))
                v_out.append(sv.n)
                assert sv.n == v_out_f[dc] and v_xp[-1] == v_xp_f[dc] and v_scan[-1] == v_scan_f[dc]

        @blk.gpsimd
        def _(gpsimd):
            for dc in range(NDC):
                gpsimd.wait_ge(sv.h, v_out_f[dc])
                sst.dinc(gpsimd.dma_start(out=K.attn0s[dc], in_=outb))
                d_spill.append(sst.n)
                assert sst.n == d_spill_f[dc]


def phase_p2(K):
    nc = K.nc
    ld, sv, ss, st, sst = K.sld, K.sv, K.ss, K.st, K.sst
    with (
        nc.sbuf_tensor("p2_gw", [128, 2048], F32) as gwb_,
        nc.sbuf_tensor("p2_uw", [128, 2048], F32) as uwb_,
        nc.sbuf_tensor("p2_gwb", [128, 2, 2048], BF16) as gwbf,
        nc.sbuf_tensor("p2_uwb", [128, 2, 2048], BF16) as uwbf,
        nc.sbuf_tensor("p2_sg", [128, T], F32) as sgb_,
        nc.sbuf_tensor("p2_hid", [128, 2, T], BF16) as hidb,
        nc.psum_tensor("p2_pg", [128, 2048], F32) as pG,
        nc.psum_tensor("p2_pu", [128, 2048], F32) as pU,
        nc.Block() as blk,
    ):
        gwb, uwb, sgb = gwb_[:, :], uwb_[:, :], sgb_[:, :]
        ld0, ss0, st0p, sv0, sst0 = ld.n, ss.n, st.n, sv.n, sst.n
        s_cast = [ss0 + 2 * hc + 1 for hc in range(NHC)]
        s_silu = [ss0 + 2 * hc + 2 for hc in range(NHC)]
        t_g = [st0p + 2 * hc + 1 for hc in range(NHC)]
        t_u = [st0p + 2 * hc + 2 for hc in range(NHC)]
        v_hid = [sv0 + hc + 1 for hc in range(NHC)]
        d_sp = [sst0 + 16 * (hc + 1) for hc in range(NHC)]

        @blk.sync
        def _(sync):
            for hc in range(NHC):
                if hc >= 1:
                    sync.wait_ge(ss.h, s_cast[hc - 1])
                ld.dinc(sync.dma_start(out=gwb, in_=K.wg0[hc]))
                ld.dinc(sync.dma_start(out=uwb, in_=K.wu0[hc]))

        @blk.scalar
        def _(scalar):
            for hc in range(NHC):
                scalar.wait_ge(ld.h, ld0 + 32 * (hc + 1))
                scalar.copy(out=gwbf[:, hc % 2, :], in_=gwb)
                ss.inc(scalar.copy(out=uwbf[:, hc % 2, :], in_=uwb))
                scalar.wait_ge(st.h, t_g[hc])
                if hc >= 1:
                    scalar.wait_ge(sv.h, v_hid[hc - 1])
                ss.inc(scalar.activation(out=sgb, in_=pG[:, :], func=AF.Silu))

        @blk.tensor
        def _(tensor):
            for hc in range(NHC):
                tensor.wait_ge(ss.h, s_cast[hc])
                if hc >= 1:
                    tensor.wait_ge(ss.h, s_silu[hc - 1])
                for tc in range(4):
                    inst = None
                    for g in range(NDC):
                        inst = tensor.matmul(
                            out=pG[:, tc * 512:(tc + 1) * 512],
                            lhsT=gwbf[:, hc % 2, g * 128:(g + 1) * 128],
                            rhs=K.xn_chunk(g, tc * 512, (tc + 1) * 512),
                            start=(g == 0), stop=(g == NDC - 1))
                st.inc(inst)
                if hc >= 1:
                    tensor.wait_ge(sv.h, v_hid[hc - 1])
                for tc in range(4):
                    inst = None
                    for g in range(NDC):
                        inst = tensor.matmul(
                            out=pU[:, tc * 512:(tc + 1) * 512],
                            lhsT=uwbf[:, hc % 2, g * 128:(g + 1) * 128],
                            rhs=K.xn_chunk(g, tc * 512, (tc + 1) * 512),
                            start=(g == 0), stop=(g == NDC - 1))
                st.inc(inst)

        @blk.vector
        def _(vector):
            for hc in range(NHC):
                vector.wait_ge(st.h, t_u[hc])
                vector.wait_ge(ss.h, s_silu[hc])
                if hc >= 2:
                    vector.wait_ge(sst.h, d_sp[hc - 2])
                sv.inc(vector.tensor_mul(hidb[:, hc % 2, :], pU[:, :], sgb))

        @blk.gpsimd
        def _(gpsimd):
            for hc in range(NHC):
                gpsimd.wait_ge(sv.h, v_hid[hc])
                sst.dinc(gpsimd.dma_start(out=K.hid0s[hc], in_=hidb[:, hc % 2, :]))


def phase_p3(K):
    nc = K.nc
    ld, sv, ss, st, sst = K.sld, K.sv, K.ss, K.st, K.sst
    NSET = 8
    with (
        nc.sbuf_tensor("p3_wd", [128, 2, 256], F32) as wds_,
        nc.sbuf_tensor("p3_wdb", [128, 2, 256], BF16) as wdb,
        nc.sbuf_tensor("p3_hid", [128, 2, T], BF16) as hbuf,
        nc.sbuf_tensor("p3_ab", [128, 2, T], F32) as abuf,
        nc.sbuf_tensor("p3_slab", [128, 2, T], F32) as slab,
        nc.sbuf_tensor("p3_x0c", [128, 2, T], F32) as x0c,
        nc.sbuf_tensor("p3_xou", [128, 2, 512], F32) as xou,
        nc.sbuf_tensor("p3_bn", [128, NTT * 96], F32) as p3bn,
        nc.sbuf_tensor("p3_st", [128, 48], F32) as stat,
        nc.psum_tensor("p3_pa", [128, 2048], F32) as pA,
        nc.psum_tensor("p3_pb", [128, 2048], F32) as pB,
        nc.Block() as blk,
    ):
        wds = wds_
        pAB = [pA, pB]
        ld0, ss0, st0p, sv0, sst0 = ld.n, ss.n, st.n, sv.n, sst.n
        def f_ld(s, hc):
            return ld0 + 16 * (s * 132 + 4 + 2 * (hc + 1))
        def f_ld_pre(s):
            return ld0 + 16 * (s * 132 + 4)
        def f_cast(s, hc):
            return ss0 + s * 64 + hc + 1
        def f_mm(s, hc):
            return st0p + s * 72 + hc + 1
        def f_tr(s, grp):
            return st0p + s * 72 + 64 + grp + 1
        def f_ev(s):
            return sv0 + s * 18 + 2
        def f_grp_add(s, grp):
            return sv0 + s * 18 + 2 + 2 * grp + 1
        def f_grp_bn(s, grp):
            return sv0 + s * 18 + 2 + 2 * grp + 2
        def f_x1(s, grp):
            return sst0 + 16 * (s * 8 + grp + 1)

        @blk.sync
        def _(sync):
            for s in range(NSET):
                for m in range(2):
                    dc = 2 * s + m
                    if s >= 1:
                        sync.wait_ge(sv.h, f_ev(s - 1))
                        sync.wait_ge(sst.h, f_x1(s - 1, 7))
                    ld.dinc(sync.dma_start(out=abuf[:, m, :], in_=K.attn0s[dc]))
                for m in range(2):
                    dc = 2 * s + m
                    src_ap = K.x0[:, dc * 128:(dc + 1) * 128]
                    src3 = src_ap.rearrange("(tt p) d -> p tt d", p=128)
                    dst3 = x0c[:, m, :].rearrange("p (tt d) -> p tt d", d=128)
                    ld.dinc(sync.dma_start(out=dst3, in_=src3))
                for hc in range(NHC):
                    if hc >= 2:
                        sync.wait_ge(ss.h, f_cast(s, hc - 2))
                        sync.wait_ge(st.h, f_mm(s, hc - 2))
                    ld.dinc(sync.dma_start(out=wds[:, hc % 2, :],
                                           in_=K.wd0[hc, :, 2 * s:2 * s + 2, :]))
                    ld.dinc(sync.dma_start(out=hbuf[:, hc % 2, :], in_=K.hid0s[hc]))

        @blk.scalar
        def _(scalar):
            for s in range(NSET):
                for hc in range(NHC):
                    scalar.wait_ge(ld.h, f_ld(s, hc))
                    if hc >= 2:
                        scalar.wait_ge(st.h, f_mm(s, hc - 2))
                    ss.inc(scalar.copy(out=wdb[:, hc % 2, :], in_=wds[:, hc % 2, :]))
            scalar.wait_ge(sv.h, sv0 + NSET * 18 + 4)
            ss.inc(scalar.activation(out=p3bn[:, 16:32], in_=p3bn[:, 0:16],
                                     func=AF.Sqrt))

        @blk.tensor
        def _(tensor):
            for s in range(NSET):
                for hc in range(NHC):
                    tensor.wait_ge(ss.h, f_cast(s, hc))
                    tensor.wait_ge(ld.h, f_ld(s, hc))
                    if hc == 0 and s >= 1:
                        tensor.wait_ge(sv.h, f_grp_add(s - 1, 7))
                    inst = None
                    for m in range(2):
                        for tc in range(4):
                            inst = tensor.matmul(
                                out=pAB[m][:, tc * 512:(tc + 1) * 512],
                                lhsT=wdb[:, hc % 2, m * 128:(m + 1) * 128],
                                rhs=hbuf[:, hc % 2, tc * 512:(tc + 1) * 512],
                                start=(hc == 0), stop=(hc == NHC - 1))
                    st.inc(inst)
                tensor.wait_ge(sv.h, f_ev(s))
                for grp in range(8):
                    m, g4 = divmod(grp, 4)
                    inst = None
                    for j in range(4):
                        tt = g4 * 4 + j
                        inst = tensor.matmul(
                            out=pAB[m][:, g4 * 512 + j * 128: g4 * 512 + (j + 1) * 128],
                            lhsT=slab[:, m, tt * 128:(tt + 1) * 128],
                            rhs=K.c_identf,
                            start=(j == 0), stop=(j == 3))
                    st.inc(inst)

        @blk.vector
        def _(vector):
            for s in range(NSET):
                vector.wait_ge(st.h, f_mm(s, NHC - 1))
                vector.wait_ge(ld.h, f_ld_pre(s))
                for m in range(2):
                    sv.chain(vector, vector.tensor_add(slab[:, m, :], pAB[m][:, :],
                                                       abuf[:, m, :]))
                for grp in range(8):
                    m, g4 = divmod(grp, 4)
                    dc = 2 * s + m
                    vector.wait_ge(st.h, f_tr(s, grp))
                    if s * 8 + grp >= 2:
                        pg = s * 8 + grp - 2
                        vector.wait_ge(sst.h, f_x1(pg // 8, pg % 8))
                    sv.inc(vector.tensor_add(xou[:, grp % 2, :],
                                             pAB[m][:, g4 * 512:(g4 + 1) * 512],
                                             x0c[:, m, g4 * 512:(g4 + 1) * 512]))
                    vector.wait_ge(sv.h, sv.n)
                    inst = None
                    for j in range(4):
                        tt = g4 * 4 + j
                        inst = vector.bn_stats(
                            out=p3bn[:, tt * 96 + dc * 6: tt * 96 + dc * 6 + 6],
                            in_=xou[:, grp % 2, j * 128:(j + 1) * 128])
                    sv.inc(inst)
            mv2 = stat[:, 0:32]
            inst = None
            for tt in range(NTT):
                inst = vector.bn_aggr(out=stat[:, 2 * tt:2 * tt + 2],
                                      in_=p3bn[:, tt * 96:(tt + 1) * 96])
            sv.chain(vector, inst)
            mv3 = mv2.rearrange("p (t two) -> p t two", two=2)
            sv.chain(vector, vector.tensor_copy(out=K.mean1[:, :], in_=mv3[:, :, 0:1]))
            ve = stat[:, 32:48]
            sv.chain(vector, vector.tensor_scalar(
                out=ve, in0=mv3[:, :, 1:2], scalar1=LN_EPS,
                scalar2=None, op0=ALU.add))
            sv.chain(vector, vector.reciprocal(out=p3bn[:, 0:16], in_=ve))
            assert sv.n == sv0 + NSET * 18 + 4, (sv.n,)
            vector.wait_ge(ss.h, ss0 + NSET * 64 + 1)
            r0 = p3bn[:, 16:32]
            tmp = p3bn[:, 32:48]
            sv.chain(vector, vector.tensor_mul(tmp, r0, r0))
            sv.chain(vector, vector.tensor_mul(tmp, tmp, ve))
            sv.chain(vector, vector.tensor_scalar(
                out=tmp, in0=tmp,
                scalar1=-0.5, scalar2=1.5, op0=ALU.mult, op1=ALU.add))
            sv.chain(vector, vector.tensor_mul(K.rstd1[:, :], r0, tmp))

        @blk.gpsimd
        def _(gpsimd):
            for s in range(NSET):
                for grp in range(8):
                    m, g4 = divmod(grp, 4)
                    dc = 2 * s + m
                    gpsimd.wait_ge(sv.h, f_grp_bn(s, grp))
                    dst = K.x1s[g4 * 512:(g4 + 1) * 512, dc * 128:(dc + 1) * 128]
                    dst3 = dst.rearrange("(g p) d -> p g d", p=128)
                    src3 = xou[:, grp % 2, :].rearrange("p (g d) -> p g d", d=128)
                    sst.dinc(gpsimd.dma_start(out=dst3, in_=src3))


def phase_p4(K):
    nc = K.nc
    ld, sv, ss, st, sst = K.sld, K.sv, K.ss, K.st, K.sst
    with (
        nc.sbuf_tensor("p4_tb", [128, 2, T], F32) as tbuf,
        nc.sbuf_tensor("p4_xn", [128, T], F32) as xnt_,
        nc.sbuf_tensor("p4_y", [128, T], F32) as yb_,
        nc.sbuf_tensor("p4_t1", [128, T], F32) as t1_,
        nc.sbuf_tensor("p4_t2", [128, T], F32) as t2_,
        nc.sbuf_tensor("p4_xnb", [128, 2, T], BF16) as xnbf,
        nc.sbuf_tensor("p4_xq", [128, 2, T], BF16) as xqt,
        nc.sbuf_tensor("p4_xqs", [128, 4, 512], BF16) as xqstg,
        nc.sbuf_tensor("p4_st", [128, 8], F32) as stat,
        nc.psum_tensor("p4_ptr0", [128, 512], F32) as ptr0,
        nc.psum_tensor("p4_ptr1", [128, 512], F32) as ptr1,
        nc.Block() as blk,
    ):
        xnt, yb, t1, t2 = xnt_[:, :], yb_[:, :], t1_[:, :], t2_[:, :]
        ptrs = [ptr0, ptr1]
        ld0, ss0, st0p, sv0 = ld.n, ss.n, st.n, sv.n
        v_xn = [sv0 + 18 * tt + 1 for tt in range(NTT)]
        v_clip = [sv0 + 18 * tt + 10 for tt in range(NTT)]
        s_cast = [ss0 + tt + 1 for tt in range(NTT)]
        t_grp = [st0p + i + 1 for i in range(8 * NTT)]
        sst0 = sst.n
        v_evac = {}
        d_xq = {(tt, g4): sst0 + 16 * (4 * tt + g4 + 1)
                for tt in range(NTT) for g4 in range(4)}

        def emit_evacs(vector, tt):
            for g in range(8):
                i = 8 * tt + g
                g4 = g % 4
                vector.wait_ge(st.h, t_grp[i])
                if g < 4:
                    dst = K.xnf3[:, 4 * g4:4 * g4 + 4, tt * 128:(tt + 1) * 128]
                else:
                    if tt >= 1:
                        vector.wait_ge(sst.h, d_xq[(tt - 1, g4)])
                    dst = xqstg[:, g4, :]
                sv.inc(vector.tensor_copy(
                    out=dst, in_=ptrs[i % 2].rearrange("p (a b) -> p a b", a=4)))
                v_evac[i] = sv.n

        @blk.sync
        def _(sync):
            for tt in range(NTT):
                if tt >= 2:
                    sync.wait_ge(sv.h, v_xn[tt - 1])
                ld.dinc(sync.dma_start(out=tbuf[:, tt % 2, :],
                                       in_=K.x1s[tt * 128:(tt + 1) * 128, :]))

        @blk.scalar
        def _(scalar):
            for tt in range(NTT):
                scalar.wait_ge(sv.h, v_xn[tt])
                if tt >= 2:
                    scalar.wait_ge(st.h, t_grp[8 * (tt - 2) + 3])
                ss.inc(scalar.copy(out=xnbf[:, tt % 2, :], in_=xnt))

        @blk.vector
        def _(vector):
            for tt in range(NTT):
                if tt >= 1:
                    emit_evacs(vector, tt - 1)
                vector.wait_ge(ld.h, ld0 + 16 * (tt + 1))
                if tt >= 1:
                    vector.wait_ge(ss.h, s_cast[tt - 1])
                sv.chain(vector, vector.tensor_scalar(
                    out=xnt, in0=tbuf[:, tt % 2, :],
                    scalar1=K.mean1[:, tt:tt + 1], scalar2=K.rstd1[:, tt:tt + 1],
                    op0=ALU.subtract, op1=ALU.mult))
                sv.chain(vector, vector.tensor_reduce(
                    out=stat[:, 0:1], in_=xnt, axis=AXX, op=ALU.max,
                    apply_absolute_value=True))
                sv.chain(vector, vector.tensor_scalar_max(stat[:, 1:2], stat[:, 0:1], Q_EPS))
                sv.chain(vector, vector.reciprocal(out=stat[:, 2:3], in_=stat[:, 1:2]))
                sv.chain(vector, vector.tensor_scalar_mul(K.sact1[:, tt:tt + 1],
                                                          stat[:, 2:3], 128.0))
                sv.chain(vector, vector.tensor_scalar_mul(K.rs1[:, tt:tt + 1],
                                                          stat[:, 1:2], 1.0 / 128.0))
                sv.chain(vector, vector.tensor_scalar_mul(yb, xnt,
                                                          K.sact1[:, tt:tt + 1]))
                vector.tensor_scalar(out=t1, in0=yb, scalar1=C32, scalar2=-C32,
                                     op0=ALU.add, op1=ALU.add)
                sv.chain(vector, vector.tensor_scalar(
                    out=t2, in0=yb, scalar1=C64, scalar2=-C64,
                    op0=ALU.add, op1=ALU.add))
                sv.chain(vector, vector.scalar_tensor_tensor(
                    out=yb, in0=t1, scalar=K.kk, in1=t2, op0=ALU.mult, op1=ALU.add))
                if tt >= 2:
                    vector.wait_ge(st.h, t_grp[8 * (tt - 2) + 7])
                sv.chain(vector, vector.tensor_scalar(
                    out=xqt[:, tt % 2, :], in0=yb, scalar1=-128.0, scalar2=127.0,
                    op0=ALU.max, op1=ALU.min))
                assert sv.n == v_clip[tt], (sv.n, v_clip[tt])
            emit_evacs(vector, NTT - 1)

        @blk.tensor
        def _(tensor):
            for tt in range(NTT):
                for g in range(8):
                    i = 8 * tt + g
                    if g == 0:
                        tensor.wait_ge(ss.h, s_cast[tt])
                    if g == 4:
                        tensor.wait_ge(sv.h, v_clip[tt])
                    if i >= 2:
                        tensor.wait_ge(sv.h, v_evac[i - 2])
                    srcb = xnbf if g < 4 else xqt
                    g4 = g % 4
                    pt = ptrs[i % 2]
                    inst = None
                    for j in range(4):
                        dc = g4 * 4 + j
                        inst = tensor.matmul(out=pt[:, j * 128:(j + 1) * 128],
                                             lhsT=srcb[:, tt % 2, dc * 128:(dc + 1) * 128],
                                             rhs=K.c_identb,
                                             start=(j == 0), stop=(j == 3))
                    st.inc(inst)

        @blk.gpsimd
        def _(gpsimd):
            for tt in range(NTT):
                for g4 in range(4):
                    i = 8 * tt + 4 + g4
                    gpsimd.wait_ge(sv.h, v_evac[i])
                    dst = K.xqs[g4 * 4:(g4 + 1) * 4, :, tt * 128:(tt + 1) * 128]
                    sst.dinc(gpsimd.dma_start(
                        out=dst.rearrange("a p b -> p a b"),
                        in_=xqstg[:, g4, :].rearrange("p (a b) -> p a b", a=4)))
                    assert sst.n == d_xq[(tt, g4)]


def phase_p5(K):
    nc = K.nc
    ld, sv, ss, st, sst = K.sld, K.sv, K.ss, K.st, K.sst
    SC = 128.0 ** -0.5
    with (
        nc.sbuf_tensor("p5_knf", [128, NKV * T], BF16) as knf_,
        nc.sbuf_tensor("p5_vtf", [128, NKV * T], BF16) as vtf_,
        nc.sbuf_tensor("p5_wb", [128, 2048], F32) as wb_,
        nc.sbuf_tensor("p5_wbf", [128, 2048], BF16) as wbf_,
        nc.sbuf_tensor("p5_sq", [128, T], BF16) as sqb_,
        nc.sbuf_tensor("p5_rbc", [128, T], F32) as rbc_,
        nc.sbuf_tensor("p5_qn", [128, T], BF16) as qnh_,
        nc.sbuf_tensor("p5_pb", [128, T], BF16) as pbb_,
        nc.sbuf_tensor("p5_pta", [128, NTT * 512], BF16) as ptA_,
        nc.sbuf_tensor("p5_ob", [128, 2, 512], BF16) as ob_,
        nc.sbuf_tensor("p5_vt", [128, T], BF16) as vtmp_,
        nc.sbuf_tensor("p5_st", [128, 8], F32) as stat_,
        nc.psum_tensor("p5_pa", [128, 2048], F32) as pA,
        nc.psum_tensor("p5_pb2", [128, 2048], F32) as pB,
    ):
        knf, vtf, wb, wbf = knf_[:, :], vtf_[:, :], wb_[:, :], wbf_[:, :]
        sqb, rbc, qnh = sqb_[:, :], rbc_[:, :], qnh_[:, :]
        K.d_knf, K.d_vtf, K.d_qnh = knf, vtf, qnh
        pbf, vtmp, stat = pbb_[:, :], vtmp_[:, :], stat_[:, :]
        ob = ob_
        ptA3 = ptA_[:, :].rearrange("p (s c) -> p s c", c=512)

        def project(blk, w4, oc):
            @blk.sync
            def _(sync):
                ld.dinc(sync.dma_start(out=wb, in_=w4[oc]))
            ld_w = ld.n

            @blk.scalar
            def _(scalar):
                scalar.wait_ge(ld.h, ld_w)
                ss.inc(scalar.copy(out=wbf, in_=wb))
            cast_done = ss.n

            @blk.tensor
            def _(tensor):
                tensor.wait_ge(ss.h, cast_done)
                inst = None
                for tc in range(4):
                    for g in range(NDC):
                        inst = tensor.matmul(
                            out=pA[:, tc * 512:(tc + 1) * 512],
                            lhsT=wbf[:, g * 128:(g + 1) * 128],
                            rhs=K.xn_chunk(g, tc * 512, (tc + 1) * 512),
                            start=(g == 0), stop=(g == NDC - 1))
                st.inc(inst)
            return st.n

        def rms_rbc(blk, mm_done, fold):
            @blk.scalar
            def _(scalar):
                scalar.wait_ge(st.h, mm_done)
                ss.inc(scalar.activation(out=sqb, in_=pA[:, :], func=AF.Square))
            sq_done = ss.n

            @blk.tensor
            def _(tensor):
                tensor.wait_ge(ss.h, sq_done)
                for tc in range(4):
                    st.inc(tensor.matmul(
                        out=pB[0:1, tc * 512:(tc + 1) * 512],
                        lhsT=K.c_onesb,
                        rhs=sqb[:, tc * 512:(tc + 1) * 512],
                        start=True, stop=True))
            ssq_done = st.n

            @blk.vector
            def _(vector):
                vector.wait_ge(st.h, ssq_done)
                sv.inc(vector.tensor_copy(out=rbc_[0:1, :], in_=pB[0:1, :]))
            row_done = sv.n

            @blk.sync
            def _(sync):
                sync.wait_ge(sv.h, row_done)
                for k2 in [1, 2, 4, 8, 16, 32, 64]:
                    ld.dinc(sync.dma_start(out=rbc_[k2:2 * k2, :],
                                           in_=rbc_[0:k2, :]))
                    sync.wait_ge(ld.h, ld.n)
            bc_done = ld.n

            @blk.scalar
            def _(scalar):
                scalar.wait_ge(ld.h, bc_done)
                ss.inc(scalar.activation(out=rbc, in_=rbc, func=AF.Sqrt,
                                         scale=1.0 / 128.0, bias=K.c_epsr))
            sd_done = ss.n

            @blk.vector
            def _(vector):
                vector.wait_ge(ss.h, sd_done)
                sv.chain(vector, vector.reciprocal(out=rbc, in_=rbc))
                if fold:
                    sv.chain(vector, vector.tensor_scalar(
                        out=rbc, in0=rbc, scalar1=K.c_sqrt2, scalar2=0.0625,
                        op0=ALU.mult, op1=ALU.mult))
            return sv.n

        for kv in range(NKV):
            with nc.Block() as blk:
                mm_done = project(blk, K.wk, kv)
                rbc_done = rms_rbc(blk, mm_done, False)

                @blk.vector
                def _(vector):
                    vector.wait_ge(sv.h, rbc_done)
                    sv.inc(vector.scalar_tensor_tensor(
                        out=knf[:, kv * T:(kv + 1) * T], in0=pA[:, :],
                        scalar=K.c_gk, in1=rbc, op0=ALU.mult, op1=ALU.mult))

            with nc.Block() as blk:
                mm_done = project(blk, K.wv, kv)

                @blk.vector
                def _(vector):
                    vector.wait_ge(st.h, mm_done)
                    sv.inc(vector.tensor_copy(out=vtmp, in_=pA[:, :]))
                ev_done = sv.n

                tr_marks = []

                @blk.tensor
                def _(tensor):
                    tensor.wait_ge(sv.h, ev_done)
                    for jg in range(4):
                        if jg >= 1:
                            tensor.wait_ge(sv.h, ev_done + jg)
                        inst = None
                        for j in range(4):
                            tt = jg * 4 + j
                            inst = tensor.matmul(
                                out=pB[:, j * 128:(j + 1) * 128],
                                lhsT=vtmp[:, tt * 128:(tt + 1) * 128],
                                rhs=K.c_identb, start=(j == 0), stop=(j == 3))
                        st.inc(inst)
                        tr_marks.append(st.n)

                @blk.vector
                def _(vector):
                    for jg in range(4):
                        vector.wait_ge(st.h, tr_marks[jg])
                        sv.inc(vector.tensor_copy(
                            out=vtf[:, kv * T + jg * 512:kv * T + (jg + 1) * 512],
                            in_=pB[:, 0:512]))

        for h in range(NH):
            kv = h // 4
            with nc.Block() as blk:
                mm_done = project(blk, K.wq, h)
                rbc_done = rms_rbc(blk, mm_done, True)

                @blk.vector
                def _(vector):
                    vector.wait_ge(sv.h, rbc_done)
                    sv.inc(vector.scalar_tensor_tensor(
                        out=qnh, in0=pA[:, :],
                        scalar=K.c_gq, in1=rbc, op0=ALU.mult, op1=ALU.mult))
                qn_done = sv.n

                st1, sv1, ss1, sst1 = st.n, qn_done, ss.n, sst.n
                tpos, vpos, spos, gpos = st1, sv1, ss1, sst1
                sc_done = [0] * NTT
                trg = {}
                av = [0] * NTT
                msk_done = [0] * NTT
                nrm_done = [0] * NTT
                evc = {}
                oev = [0] * NTT
                exp_done = [0] * NTT
                spd = [0] * 4
                for i in range(NTT):
                    njg = (i + 4) // 4
                    tpos += 1; sc_done[i] = tpos
                    for jg in range(njg):
                        tpos += 1; trg[(i, jg)] = tpos
                    tpos += 1; av[i] = tpos
                    vpos += 2; msk_done[i] = vpos
                    vpos += 3; nrm_done[i] = vpos
                    for jg in range(njg):
                        vpos += 1; evc[(i, jg)] = vpos
                    vpos += 1; oev[i] = vpos
                    spos += 1; exp_done[i] = spos
                for g in range(4):
                    gpos += 16; spd[g] = gpos

                @blk.tensor
                def _(tensor):
                    tensor.wait_ge(sv.h, qn_done)
                    for i in range(NTT):
                        Ltk = 128 * (i + 1)
                        njg = (i + 4) // 4
                        if i >= 1:
                            tensor.wait_ge(ss.h, exp_done[i - 1])
                        inst = None
                        for ck in range((Ltk + 511) // 512):
                            c1 = min(Ltk, (ck + 1) * 512)
                            inst = tensor.matmul(
                                out=pA[:, ck * 512:c1],
                                lhsT=qnh[:, i * 128:(i + 1) * 128],
                                rhs=knf[:, kv * T + ck * 512:kv * T + c1],
                                start=True, stop=True)
                        st.inc(inst)
                        assert st.n == sc_done[i]
                        for jg in range(njg):
                            tensor.wait_ge(sv.h, nrm_done[i])
                            if (i, jg) != (0, 0):
                                prev = (i, jg - 1) if jg else (i - 1, (i + 3) // 4 - 1)
                                tensor.wait_ge(sv.h, evc[prev])
                            inst = None
                            nj = min(4, i + 1 - jg * 4)
                            for j in range(nj):
                                tk = jg * 4 + j
                                inst = tensor.matmul(
                                    out=pB[:, j * 128:(j + 1) * 128],
                                    lhsT=pbf[:, tk * 128:(tk + 1) * 128],
                                    rhs=K.c_identb, start=(j == 0), stop=(j == nj - 1))
                            st.inc(inst)
                            assert st.n == trg[(i, jg)]
                        tensor.wait_ge(sv.h, evc[(i, njg - 1)])
                        if i >= 1:
                            tensor.wait_ge(sv.h, oev[i - 1])
                        inst = None
                        for j in range(i + 1):
                            inst = tensor.matmul(
                                out=pB[:, 512:640],
                                lhsT=vtf[:, kv * T + j * 128:kv * T + (j + 1) * 128],
                                rhs=ptA3[:, j, 0:128],
                                start=(j == 0), stop=(j == i))
                        st.inc(inst)
                        assert st.n == av[i]

                @blk.vector
                def _(vector):
                    for i in range(NTT):
                        Ltk = 128 * (i + 1)
                        njg = (i + 4) // 4
                        vector.wait_ge(st.h, sc_done[i])
                        sv.chain(vector, vector.tensor_add(
                            pA[:, i * 128:(i + 1) * 128],
                            pA[:, i * 128:(i + 1) * 128], K.c_cmask))
                        sv.inc(vector.reduce_max(
                            out=stat[:, 0:1], in_=pA[:, 0:Ltk], axis=AXX,
                            negate=True))
                        assert sv.n == msk_done[i]
                        vector.wait_ge(ss.h, exp_done[i])
                        sv.chain(vector, vector.reduce_sum(
                            out=stat[:, 1:2], in_=pbf[:, 0:Ltk], axis=AXX))
                        sv.chain(vector, vector.reciprocal(out=stat[:, 2:3],
                                                           in_=stat[:, 1:2]))
                        sv.inc(vector.tensor_scalar_mul(
                            pbf[:, 0:Ltk], pbf[:, 0:Ltk], stat[:, 2:3]))
                        assert sv.n == nrm_done[i]
                        for jg in range(njg):
                            vector.wait_ge(st.h, trg[(i, jg)])
                            if i >= 1:
                                vector.wait_ge(st.h, av[i - 1])
                            sv.inc(vector.tensor_copy(
                                out=ptA3[:, jg * 4:jg * 4 + 4, 0:128],
                                in_=pB[:, 0:512].rearrange("p (a b) -> p a b", a=4)))
                            assert sv.n == evc[(i, jg)]
                        vector.wait_ge(st.h, av[i])
                        if i >= 8:
                            vector.wait_ge(sst.h, spd[i // 4 - 2])
                        sv.inc(vector.tensor_copy(
                            out=ob[:, (i // 4) % 2, (i % 4) * 128:(i % 4 + 1) * 128],
                            in_=pB[:, 512:640]))
                        assert sv.n == oev[i]

                @blk.scalar
                def _(scalar):
                    for i in range(NTT):
                        Ltk = 128 * (i + 1)
                        scalar.wait_ge(sv.h, msk_done[i])
                        if i >= 1:
                            njgp = (i + 3) // 4
                            scalar.wait_ge(st.h, trg[(i - 1, njgp - 1)])
                        ss.inc(scalar.activation(out=pbf[:, 0:Ltk], in_=pA[:, 0:Ltk],
                                                 func=AF.Exp, bias=stat[:, 0:1],
                                                 scale=1.0))
                        assert ss.n == exp_done[i]

                @blk.gpsimd
                def _(gpsimd):
                    for g in range(4):
                        gpsimd.wait_ge(sv.h, oev[4 * g + 3])
                        sst.dinc(gpsimd.dma_start(
                            out=K.attn1s[h][:, g * 512:(g + 1) * 512],
                            in_=ob[:, g % 2, :]))
                        assert sst.n == spd[g]


def phase_p6(K):
    nc = K.nc
    ld, sv, ss, st, sst = K.sld, K.sv, K.ss, K.st, K.sst
    with (
        nc.sbuf_tensor("p6_at", [128, NH * T], BF16) as atf_,
        nc.sbuf_tensor("p6_wb", [128, 2048], F32) as wb_,
        nc.sbuf_tensor("p6_wbf", [128, 2, 2048], BF16) as wbf,
        nc.sbuf_tensor("p6_out", [128, 2, T], F32) as outb,
        nc.psum_tensor("p6_pa", [128, 2048], F32) as pA,
        nc.psum_tensor("p6_pb", [128, 2048], F32) as pB,
        nc.Block() as blk,
    ):
        atf, wb = atf_[:, :], wb_[:, :]
        ld0, ss0, st0p, sv0, sst0 = ld.n, ss.n, st.n, sv.n, sst.n
        pAB = [pA, pB]
        s_cast = [ss0 + oc + 1 for oc in range(NDC)]
        t_mm = [st0p + oc + 1 for oc in range(NDC)]
        v_ev = [sv0 + oc + 1 for oc in range(NDC)]
        d_sp = [sst0 + 16 * (oc + 1) for oc in range(NDC)]

        @blk.sync
        def _(sync):
            for ic in range(NH):
                ld.dinc(sync.dma_start(out=atf_[:, ic * T:(ic + 1) * T],
                                       in_=K.attn1s[ic]))
            for oc in range(NDC):
                if oc >= 1:
                    sync.wait_ge(ss.h, s_cast[oc - 1])
                ld.dinc(sync.dma_start(out=wb, in_=K.wo[oc]))

        @blk.scalar
        def _(scalar):
            for oc in range(NDC):
                scalar.wait_ge(ld.h, ld0 + 16 * (NH + oc + 1))
                if oc >= 2:
                    scalar.wait_ge(st.h, t_mm[oc - 2])
                ss.inc(scalar.copy(out=wbf[:, oc % 2, :], in_=wb))

        @blk.tensor
        def _(tensor):
            for oc in range(NDC):
                tensor.wait_ge(ss.h, s_cast[oc])
                tensor.wait_ge(ld.h, ld0 + 16 * NH)
                if oc >= 2:
                    tensor.wait_ge(sv.h, v_ev[oc - 2])
                pX = pAB[oc % 2]
                inst = None
                for tc in range(4):
                    for g in range(NH):
                        inst = tensor.matmul(
                            out=pX[:, tc * 512:(tc + 1) * 512],
                            lhsT=wbf[:, oc % 2, g * 128:(g + 1) * 128],
                            rhs=atf[:, g * T + tc * 512:g * T + (tc + 1) * 512],
                            start=(g == 0), stop=(g == NH - 1))
                st.inc(inst)

        @blk.vector
        def _(vector):
            for oc in range(NDC):
                vector.wait_ge(st.h, t_mm[oc])
                if oc >= 2:
                    vector.wait_ge(sst.h, d_sp[oc - 2])
                sv.inc(vector.tensor_copy(out=outb[:, oc % 2, :],
                                          in_=pAB[oc % 2][:, :]))

        @blk.gpsimd
        def _(gpsimd):
            for oc in range(NDC):
                gpsimd.wait_ge(sv.h, v_ev[oc])
                sst.dinc(gpsimd.dma_start(out=K.woS[oc], in_=outb[:, oc % 2, :]))


def phase_pq(K):
    nc = K.nc
    ld, sv, ss, st, sst = K.sld, K.sv, K.ss, K.st, K.sst
    trips = [(K.wg1, K.wg1q, K.iswg), (K.wu1, K.wu1q, K.iswu), (K.wd1, K.wd1q, K.iswd)]
    with (
        nc.sbuf_tensor("pq_wb", [128, 2, 2048], F32) as wbb,
        nc.sbuf_tensor("pq_acc", [128, 64], F32) as acc_,
        nc.sbuf_tensor("pq_s1", [128, 2, 2048], F32) as s1_,
        nc.sbuf_tensor("pq_q", [128, 2, 2048], BF16) as qb,
        nc.sbuf_tensor("pq_th", [128, 4], F32) as th_,
        nc.psum_tensor("pq_pr", [1, 512], F32) as pR,
    ):
        acc, th = acc_[:, :], th_[:, :]
        s1 = s1_[:, 0, :]
        for (wsrc, wdst, isw) in trips:
            with nc.Block() as blk:
                ld0, sv0 = ld.n, sv.n

                @blk.sync
                def _(sync):
                    for hc in range(NHC):
                        if hc >= 2:
                            sync.wait_ge(sv.h, sv0 + hc - 1)
                        ld.dinc(sync.dma_start(out=wbb[:, hc % 2, :], in_=wsrc[hc]))

                @blk.vector
                def _(vector):
                    for hc in range(NHC):
                        vector.wait_ge(ld.h, ld0 + 16 * (hc + 1))
                        sv.inc(vector.tensor_reduce(
                            out=acc[:, hc:hc + 1], in_=wbb[:, hc % 2, :],
                            axis=AXX, op=ALU.add, apply_absolute_value=True))
                    sv.chain(vector, vector.reduce_sum(out=acc[:, 0:1], in_=acc,
                                                       axis=AXX))
                    sv.chain(vector, vector.tensor_copy(out=s1[:, 0:1], in_=acc[:, 0:1]))
                vred = sv.n

                @blk.tensor
                def _(tensor):
                    tensor.wait_ge(sv.h, vred)
                    st.inc(tensor.matmul(out=pR[0:1, 0:1], lhsT=K.c_onesf,
                                         rhs=s1[:, 0:1], start=True, stop=True))
                tmm = st.n

                @blk.vector
                def _(vector):
                    vector.wait_ge(st.h, tmm)
                    sv.chain(vector, vector.tensor_scalar(
                        out=s1[0:1, 1:2], in0=pR[0:1, 0:1],
                        scalar1=1.0 / (HID * D), scalar2=None, op0=ALU.mult))
                    sv.chain(vector, vector.tensor_scalar_max(
                        s1[0:1, 2:3], s1[0:1, 1:2], Q_EPS))
                    sv.chain(vector, vector.tensor_scalar_mul(
                        s1[0:1, 3:4], s1[0:1, 2:3], 0.5))
                vth = sv.n

                @blk.sync
                def _(sync):
                    sync.wait_ge(sv.h, vth)
                    for k2 in [1, 2, 4, 8, 16, 32, 64]:
                        ld.dinc(sync.dma_start(out=s1_[k2:2 * k2, 0, 0:4],
                                               in_=s1_[0:k2, 0, 0:4]))
                        sync.wait_ge(ld.h, ld.n)
                bcd = ld.n

                @blk.vector
                def _(vector):
                    vector.wait_ge(ld.h, bcd)
                    vector.tensor_copy(out=isw, in_=s1[:, 2:3])
                    sv.chain(vector, vector.tensor_scalar_mul(th[:, 0:1], s1[:, 3:4], 1.0))
                    sv.chain(vector, vector.tensor_scalar_mul(th[:, 1:2], s1[:, 3:4], -1.0))

            with nc.Block() as blk:
                ld0, ss0, sv0, sst0 = ld.n, ss.n, sv.n, sst.n
                s_sgn = [ss0 + 2 * hc + 2 for hc in range(NHC)]
                v_q = [sv0 + hc + 1 for hc in range(NHC)]
                d_q = [sst0 + 16 * (hc + 1) for hc in range(NHC)]

                @blk.sync
                def _(sync):
                    for hc in range(NHC):
                        if hc >= 2:
                            sync.wait_ge(ss.h, s_sgn[hc - 2])
                        ld.dinc(sync.dma_start(out=wbb[:, hc % 2, :], in_=wsrc[hc]))

                @blk.scalar
                def _(scalar):
                    for hc in range(NHC):
                        scalar.wait_ge(ld.h, ld0 + 16 * (hc + 1))
                        if hc >= 2:
                            scalar.wait_ge(sv.h, v_q[hc - 2])
                        ss.inc(scalar.activation(out=s1_[:, hc % 2, :],
                                          in_=wbb[:, hc % 2, :],
                                          func=AF.Sign, bias=th[:, 0:1]))
                        ss.inc(scalar.activation(out=qb[:, hc % 2, :],
                                                 in_=wbb[:, hc % 2, :],
                                                 func=AF.Sign, bias=th[:, 0:1],
                                                 scale=-1.0))

                @blk.vector
                def _(vector):
                    for hc in range(NHC):
                        vector.wait_ge(ss.h, s_sgn[hc])
                        if hc >= 2:
                            vector.wait_ge(sst.h, d_q[hc - 2])
                        sv.inc(vector.tensor_sub(qb[:, hc % 2, :], s1_[:, hc % 2, :],
                                                 qb[:, hc % 2, :]))

                @blk.gpsimd
                def _(gpsimd):
                    for hc in range(NHC):
                        gpsimd.wait_ge(sv.h, v_q[hc])
                        sst.dinc(gpsimd.dma_start(out=wdst[hc],
                                                  in_=qb[:, hc % 2, :]))
    with nc.Block() as blk:
        @blk.vector
        def _(vector):
            sv = K.sv
            sv.chain(vector, vector.reciprocal(out=K.tiny[:, 4:5], in_=K.iswg))
            sv.chain(vector, vector.tensor_mul(K.tiny[:, 5:6], K.tiny[:, 4:5], K.iswu))


def phase_p7(K):
    nc = K.nc
    ld, sv, ss, st, sst, sb = K.sld, K.sv, K.ss, K.st, K.sst, K.sb_
    with (
        nc.sbuf_tensor("p7_gw", [128, 2, 2048], BF16) as gwq,
        nc.sbuf_tensor("p7_uw", [128, 2, 2048], BF16) as uwq,
        nc.sbuf_tensor("p7_abc", [128, T], F32) as abc_,
        nc.sbuf_tensor("p7_gb", [128, T], F32) as gb_,
        nc.sbuf_tensor("p7_ub", [128, T], F32) as ub_,
        nc.sbuf_tensor("p7_sg", [128, T], F32) as sgb_,
        nc.sbuf_tensor("p7_hid", [128, 2, T], BF16) as hidb,
        nc.sbuf_tensor("p7_row", [16, 128], F32) as rowst,
        nc.psum_tensor("p7_pg", [128, 2048], F32) as pG,
        nc.psum_tensor("p7_pu", [128, 2048], F32) as pU,
        nc.Block() as blk,
    ):
        abc, gb, sgb, ub = abc_[:, :], gb_[:, :], sgb_[:, :], ub_[:, :]
        ld0, ss0, st0p, sv0, sst0 = ld.n, ss.n, st.n, sv.n, sst.n

        @blk.sync
        def _(sync):
            for dc in range(NDC):
                ld.dinc(sync.dma_start(out=K.xnf[:, dc * T:(dc + 1) * T],
                                       in_=K.xqs[dc]))
        xq_ld = ld.n

        @blk.tensor
        def _(tensor):
            st.inc(tensor.matmul(out=pG[0:16, 0:128], lhsT=K.rs1[:, :],
                                 rhs=K.c_identf, start=True, stop=True))
        rs_tr = st.n

        @blk.vector
        def _(vector):
            vector.wait_ge(st.h, rs_tr)
            sv.inc(vector.tensor_copy(out=rowst[0:16, :], in_=pG[0:16, 0:128]))
            sv.chain(vector, vector.tensor_scalar_max(K.tiny[:, 6:7], K.iswg, Q_EPS))
            sv.chain(vector, vector.tensor_scalar_max(K.tiny[:, 7:8], K.iswu, Q_EPS))
            sv.chain(vector, vector.tensor_scalar_max(K.tiny[:, 9:10], K.iswd, Q_EPS))
            sv.chain(vector, vector.reciprocal(out=K.tiny[:, 8:9], in_=K.tiny[:, 6:7]))
            sv.chain(vector, vector.tensor_mul(K.tiny[:, 8:9], K.tiny[:, 8:9],
                                               K.tiny[:, 7:8]))
        row_ev = sv.n

        @blk.sync
        def _(sync):
            sync.wait_ge(sv.h, row_ev)
            for i in range(16):
                ld.dinc(sync.dma_start(out=abc_[0:1, i * 128:(i + 1) * 128],
                                       in_=rowst[i:i + 1, :]))
            sync.wait_ge(ld.h, ld.n)
            for k2 in [1, 2, 4, 8, 16, 32, 64]:
                ld.dinc(sync.dma_start(out=abc_[k2:2 * k2, :], in_=abc_[0:k2, :]))
                sync.wait_ge(ld.h, ld.n)
        abc_ld = ld.n

        @blk.vector
        def _(vector):
            vector.wait_ge(ld.h, abc_ld)
            sv.chain(vector, vector.tensor_scalar(
                out=abc, in0=abc, scalar1=K.tiny[:, 6:7], scalar2=0.5,
                op0=ALU.mult, op1=ALU.mult))
        abc_done = sv.n

        ldh = [xq_ld + (abc_ld - xq_ld) + 32 * (hc + 1) for hc in range(NHC)]
        t_g = [st0p + 1 + 2 * hc + 1 for hc in range(NHC)]
        t_u = [st0p + 1 + 2 * hc + 2 for hc in range(NHC)]
        v_g = [abc_done + 3 * hc + 1 for hc in range(NHC)]
        v_u = [abc_done + 3 * hc + 2 for hc in range(NHC)]
        v_h = [abc_done + 3 * hc + 3 for hc in range(NHC)]
        s_si = [ss0 + hc + 1 for hc in range(NHC)]
        d_sp = [sst0 + 16 * (hc + 1) for hc in range(NHC)]

        @blk.sync
        def _(sync):
            for hc in range(NHC):
                if hc >= 2:
                    sync.wait_ge(st.h, t_u[hc - 2])
                ld.dinc(sync.dma_start(out=gwq[:, hc % 2, :], in_=K.wg1q[hc]))
                ld.dinc(sync.dma_start(out=uwq[:, hc % 2, :], in_=K.wu1q[hc]))
                assert ld.n == ldh[hc]

        @blk.tensor
        def _(tensor):
            tensor.wait_ge(ld.h, xq_ld)
            for hc in range(NHC):
                tensor.wait_ge(ld.h, ldh[hc])
                if hc >= 1:
                    tensor.wait_ge(sv.h, v_g[hc - 1])
                for tc in range(4):
                    inst = None
                    for g in range(NDC):
                        inst = tensor.matmul(
                            out=pG[:, tc * 512:(tc + 1) * 512],
                            lhsT=gwq[:, hc % 2, g * 128:(g + 1) * 128],
                            rhs=K.xn_chunk(g, tc * 512, (tc + 1) * 512),
                            start=(g == 0), stop=(g == NDC - 1))
                st.inc(inst)
                if hc >= 1:
                    tensor.wait_ge(sv.h, v_h[hc - 1])
                for tc in range(4):
                    inst = None
                    for g in range(NDC):
                        inst = tensor.matmul(
                            out=pU[:, tc * 512:(tc + 1) * 512],
                            lhsT=uwq[:, hc % 2, g * 128:(g + 1) * 128],
                            rhs=K.xn_chunk(g, tc * 512, (tc + 1) * 512),
                            start=(g == 0), stop=(g == NDC - 1))
                st.inc(inst)

        @blk.vector
        def _(vector):
            for hc in range(NHC):
                vector.wait_ge(st.h, t_g[hc])
                if hc >= 1:
                    vector.wait_ge(ss.h, s_si[hc - 1])
                sv.inc(vector.tensor_mul(gb, pG[:, :], abc))
                vector.wait_ge(st.h, t_u[hc])
                sv.inc(vector.tensor_mul(ub, pU[:, :], abc))
                vector.wait_ge(ss.h, s_si[hc])
                if hc >= 2:
                    vector.wait_ge(sst.h, d_sp[hc - 2])
                sv.inc(vector.scalar_tensor_tensor(
                    out=hidb[:, hc % 2, :], in0=ub, scalar=K.tiny[:, 8:9],
                    in1=sgb, op0=ALU.mult, op1=ALU.mult))

        @blk.scalar
        def _(scalar):
            for hc in range(NHC):
                scalar.wait_ge(sv.h, v_g[hc])
                if hc >= 1:
                    scalar.wait_ge(sv.h, v_h[hc - 1])
                ss.inc(scalar.activation(out=sgb, in_=gb, func=AF.Silu))

        @blk.gpsimd
        def _(gpsimd):
            for hc in range(NHC):
                gpsimd.wait_ge(sv.h, v_h[hc])
                sst.dinc(gpsimd.dma_start(out=K.hid1s[hc], in_=hidb[:, hc % 2, :]))


def phase_p8a(K):
    nc = K.nc
    ld, sv, st, sst = K.sld, K.sv, K.st, K.sst
    with (
        nc.sbuf_tensor("p8a_h", [128, 2, T], BF16) as hb,
        nc.sbuf_tensor("p8a_s", [128, 2, T], BF16) as sb2,
        nc.psum_tensor("p8a_pa", [128, 2048], F32) as pA,
        nc.psum_tensor("p8a_pb", [128, 2048], F32) as pB,
        nc.Block() as blk,
    ):
        ld0, st0p, sv0, sst0 = ld.n, st.n, sv.n, sst.n
        pAB = [pA, pB]
        t_mm = [st0p + hc + 1 for hc in range(NHC)]
        v_ev = [sv0 + hc + 1 for hc in range(NHC)]
        d_sp = [sst0 + 16 * (hc + 1) for hc in range(NHC)]

        @blk.sync
        def _(sync):
            for hc in range(NHC):
                if hc >= 2:
                    sync.wait_ge(st.h, t_mm[hc - 2])
                ld.dinc(sync.dma_start(out=hb[:, hc % 2, :], in_=K.hid1s[hc]))

        @blk.tensor
        def _(tensor):
            for hc in range(NHC):
                tensor.wait_ge(ld.h, ld0 + 16 * (hc + 1))
                if hc >= 2:
                    tensor.wait_ge(sv.h, v_ev[hc - 2])
                pX = pAB[hc % 2]
                inst = None
                for tc in range(4):
                    inst = tensor.matmul(out=pX[:, tc * 512:(tc + 1) * 512],
                                         lhsT=K.c_h128,
                                         rhs=hb[:, hc % 2, tc * 512:(tc + 1) * 512],
                                         start=True, stop=True)
                st.inc(inst)

        @blk.vector
        def _(vector):
            for hc in range(NHC):
                vector.wait_ge(st.h, t_mm[hc])
                if hc >= 2:
                    vector.wait_ge(sst.h, d_sp[hc - 2])
                sv.inc(vector.tensor_copy(out=sb2[:, hc % 2, :],
                                          in_=pAB[hc % 2][:, :]))

        @blk.gpsimd
        def _(gpsimd):
            for hc in range(NHC):
                gpsimd.wait_ge(sv.h, v_ev[hc])
                sst.dinc(gpsimd.dma_start(
                    out=K.s1s[hc * 128:(hc + 1) * 128, :],
                    in_=sb2[:, hc % 2, :]))


def phase_p8b(K):
    nc = K.nc
    ld, sv, ss, st, sst, sout, sb = (K.sld, K.sv, K.ss, K.st, K.sst, K.sout,
                                     K.sb_)
    TC = 256
    NTCX = T // TC
    from contextlib import ExitStack
    _es = ExitStack()
    Yq_ = _es.enter_context(nc.sbuf_tensor("p8_yq", [128, 64 * TC], BF16))
    wdq_ = _es.enter_context(nc.sbuf_tensor("p8_wd", [128, 16, 1024], BF16))
    sbc_ = _es.enter_context(nc.sbuf_tensor("p8_sb", [128, 512], F32))
    aYb_ = _es.enter_context(nc.sbuf_tensor("p8_ab", [128, TC], F32))
    rm2_ = _es.enter_context(nc.sbuf_tensor("p8_rm2", [128, 512], F32))
    rm_ = _es.enter_context(nc.sbuf_tensor("p8_rm", [128, TC], F32))
    yt_ = _es.enter_context(nc.sbuf_tensor("p8_yt", [128, 512], F32))
    t1_ = _es.enter_context(nc.sbuf_tensor("p8_t1", [128, 512], F32))
    t2_ = _es.enter_context(nc.sbuf_tensor("p8_t2", [128, 512], F32))
    x2f_ = _es.enter_context(nc.sbuf_tensor("p8_x2", [128, NDC * TC], F32))
    wos_ = _es.enter_context(nc.sbuf_tensor("p8_wo", [128, NDC, TC], F32))
    x1b_ = _es.enter_context(nc.sbuf_tensor("p8_x1", [128, 8, 512], F32))
    otk_ = _es.enter_context(nc.sbuf_tensor("p8_ot", [128, 2, 512], F32))
    rw_ = _es.enter_context(nc.sbuf_tensor("p8_rw", [1, 512], F32))
    sm_ = _es.enter_context(nc.sbuf_tensor("p8_sm", [128, 8], F32))
    parena = _es.enter_context(nc.psum_tensor("p8_ps", [128, 4096], F32))
    with _es:
        STb3 = K.xnf[:, 0:64 * TC].rearrange("p (a t) -> p a t", t=TC)
        Ybf = K.xnf[:, 64 * TC:128 * TC]
        Yq3 = Yq_[:, :].rearrange("p (a t) -> p a t", t=TC)
        Yq2 = Yq_[:, :]
        wdq = wdq_
        sbc, aYb, rm, rm2 = sbc_[:, :], aYb_[:, :], rm_[:, :], rm2_[:, :]
        yt, t1, t2, sm = yt_[:, :], t1_[:, :], t2_[:, :], sm_[:, :]
        x2f = x2f_[:, :]
        rw1 = rw_[0:1, :]
        psB = [parena[:, 0:512], parena[:, 512:1024]]
        pD = parena[:, 1024:3072]
        ptr = parena[:, 3072:3584]
        pRow = parena[0:1, 3584:4096]

        ld0, sb0, ss0, st0, sv0, sst0, so0 = (ld.n, sb.n, ss.n, st.n, sv.n,
                                              sst.n, sout.n)
        SB_reorg, WOSC, X1C, WDC = {}, {}, {}, {}
        SS_ev, T_p2, T_rmt, T_smt, T_k1, T_dn, T_as = {}, {}, {}, {}, {}, {}, {}
        V_rmf, V_sm, V_rw, V_bc, V_q, V_de, V_as, G_out = ({}, {}, {}, {}, {},
                                                           {}, {}, {})
        h = sst0 + 64
        SB_reorg[0] = h
        for x in range(NTCX - 1):
            h += 64
            SB_reorg[x + 1] = h
        h = so0
        for x in range(NTCX):
            for g in range(8):
                h += 16
                G_out[(x, g)] = h
        h = ld0
        for x in range(NTCX):
            for a in (1, 3, 5, 7):
                h += 16
                WDC[(x, 0, a)] = h
            for dc in range(NDC):
                h += 16
                WOSC[(x, dc)] = h
            for g in range(8):
                h += 16
                X1C[(x, g)] = h
            for a in range(9, 64, 2):
                h += 16
                WDC[(x, 0, a)] = h
            for a in range(1, 64, 2):
                h += 16
                WDC[(x, 1, a)] = h
        hs = ss0
        hb = sb0
        for x in range(NTCX):
            for ag in range(32):
                hs += 1
                SS_ev[(x, ag)] = hs
            for r in range(2):
                for a in range(0, 64, 2):
                    hb += 16
                    WDC[(x, r, a)] = hb
        h = st0
        for x in range(NTCX):
            for ag in range(32):
                h += 1
                T_p2[(x, ag)] = h
            h += 1
            T_rmt[x] = h
            h += 1
            T_smt[x] = h
            h += 1
            T_k1[x] = h
            for r in range(2):
                for a in range(NHC):
                    h += 1
                    T_dn[(x, r, a)] = h
            for g in range(8):
                h += 1
                T_as[(x, g)] = h
        h = sv0
        h += 65
        V_rmf[0] = h
        for x in range(NTCX):
            h += 7
            V_sm[x] = h
            h += 1
            V_rw[x] = h
            h += 3
            V_bc[x] = h
            for ag in range(32):
                h += 5
                V_q[(x, ag)] = h
            for dc in range(NDC):
                h += 2
                V_de[(x, dc)] = h
            for g in range(8):
                h += 1
                V_as[(x, g)] = h
            if x + 1 < NTCX:
                h += 65
                V_rmf[x + 1] = h

        def wd_prev(x, r, a):
            p = x * 128 + r * 64 + a - 16
            if p < 0:
                return None
            return (p // 128, (p % 128) // 64, p % 64)

        with nc.Block() as blk:
            @blk.gpsimd
            def _(gpsimd):
                def reorg(x):
                    for s2 in range(2):
                        for hh in range(2):
                            in_ap = bass.AP(
                                K.s1s.tensor,
                                s2 * T + hh * 32 * 128 * T + x * TC,
                                [[128 * T, 32], [2 * T, 64], [1, TC]])
                            out_ap = STb3[s2 * 64 + hh * 32:
                                          s2 * 64 + hh * 32 + 32, :, :]
                            sst.dinc(gpsimd.dma_start(out=out_ap, in_=in_ap))
                reorg(0)
                assert sst.n == SB_reorg[0]
                for x in range(NTCX):
                    if x + 1 < NTCX:
                        gpsimd.wait_ge(st.h, T_p2[(x, 31)])
                        reorg(x + 1)
                        assert sst.n == SB_reorg[x + 1]
                    for g in range(8):
                        gpsimd.wait_ge(sv.h, V_as[(x, g)])
                        half, dcg = divmod(g, 4)
                        dst = K.out[x * TC + half * 128:
                                    x * TC + (half + 1) * 128,
                                    dcg * 512:(dcg + 1) * 512]
                        sout.dinc(gpsimd.dma_start(out=dst,
                                                   in_=otk_[:, g % 2, :]))
                        assert sout.n == G_out[(x, g)]
                gpsimd.wait_ge(sout.h, sout.n)

            @blk.sync
            def _(sync):
                def wd_load(x, r, a):
                    pv = wd_prev(x, r, a)
                    if pv is not None:
                        sync.wait_ge(st.h, T_dn[pv])
                    s = (r * 64 + a) % 16
                    ld.dinc(sync.dma_start(
                        out=wdq[:, s, :].rearrange("p (j c) -> p j c", j=8),
                        in_=K.wd1q[a][:, r * 8:(r + 1) * 8, :]))
                    assert ld.n == WDC[(x, r, a)]
                for x in range(NTCX):
                    for a in (1, 3, 5, 7):
                        wd_load(x, 0, a)
                    for dc in range(NDC):
                        if x >= 1:
                            sync.wait_ge(sv.h, V_de[(x - 1, dc)])
                        ld.dinc(sync.dma_start(
                            out=wos_[:, dc, :],
                            in_=K.woS[dc][:, x * TC:(x + 1) * TC]))
                        assert ld.n == WOSC[(x, dc)]
                    for g in range(8):
                        if x >= 1:
                            sync.wait_ge(sv.h, V_as[(x - 1, g)])
                        half, dcg = divmod(g, 4)
                        src = K.x1s[x * TC + half * 128:
                                    x * TC + (half + 1) * 128,
                                    dcg * 512:(dcg + 1) * 512]
                        ld.dinc(sync.dma_start(out=x1b_[:, g, :], in_=src))
                        assert ld.n == X1C[(x, g)]
                    for a in range(9, 64, 2):
                        wd_load(x, 0, a)
                    for a in range(1, 64, 2):
                        wd_load(x, 1, a)

            @blk.scalar
            def _(scalar):
                for x in range(NTCX):
                    for ag in range(32):
                        scalar.wait_ge(st.h, T_p2[(x, ag)])
                        if x >= 1:
                            scalar.wait_ge(sv.h, V_q[(x - 1, ag)])
                        ss.inc(scalar.copy(
                            out=Ybf[:, ag * 512:(ag + 1) * 512],
                            in_=psB[ag % 2]))
                        assert ss.n == SS_ev[(x, ag)]
                    for r in range(2):
                        for a in range(0, 64, 2):
                            pv = wd_prev(x, r, a)
                            if pv is not None:
                                scalar.wait_ge(st.h, T_dn[pv])
                            s = (r * 64 + a) % 16
                            sb.dinc(scalar.dma_start(
                                out=wdq[:, s, :].rearrange(
                                    "p (j c) -> p j c", j=8),
                                in_=K.wd1q[a][:, r * 8:(r + 1) * 8, :]))
                            assert sb.n == WDC[(x, r, a)]

            @blk.tensor
            def _(tensor):
                for x in range(NTCX):
                    tensor.wait_ge(sst.h, SB_reorg[x])
                    for ag in range(32):
                        if (x, ag) not in ((0, 0), (0, 1)):
                            pa = (x, ag - 2) if ag >= 2 else (x - 1, ag + 30)
                            tensor.wait_ge(ss.h, SS_ev[pa])
                        inst = None
                        for j in range(2):
                            inst = tensor.matmul(
                                out=psB[ag % 2][:, j * TC:(j + 1) * TC],
                                lhsT=K.c_h64b, rhs=STb3[:, 2 * ag + j, :],
                                start=(j == 0), stop=(j == 1))
                        st.inc(inst)
                        assert st.n == T_p2[(x, ag)]
                    tensor.wait_ge(sv.h, V_rmf[x])
                    inst = None
                    for j in range(2):
                        inst = tensor.matmul(out=ptr[:, j * 128:(j + 1) * 128],
                                             lhsT=rm[:, j * 128:(j + 1) * 128],
                                             rhs=K.c_identf,
                                             start=(j == 0), stop=(j == 1))
                    st.inc(inst)
                    assert st.n == T_rmt[x]
                    tensor.wait_ge(sv.h, V_sm[x])
                    inst = None
                    for c in range(4):
                        inst = tensor.matmul(
                            out=pRow[0:1, c * 128:(c + 1) * 128],
                            lhsT=sm[:, 4 + c:5 + c], rhs=K.c_identf,
                            start=(c == 0), stop=(c == 3))
                    st.inc(inst)
                    assert st.n == T_smt[x]
                    tensor.wait_ge(sv.h, V_rw[x])
                    inst = None
                    for c in range(4):
                        inst = tensor.matmul(
                            out=ptr[:, c * 128:(c + 1) * 128],
                            lhsT=K.c_onesr, rhs=rw1[0:1, c * 128:(c + 1) * 128],
                            start=(c == 0), stop=(c == 3))
                    st.inc(inst)
                    assert st.n == T_k1[x]
                    for r in range(2):
                        for a in range(NHC):
                            wsem = sb if a % 2 == 0 else ld
                            tensor.wait_ge(wsem.h, WDC[(x, r, a)])
                            tensor.wait_ge(sv.h, V_q[(x, a // 2)])
                            if r == 0 and a == 0 and x >= 1:
                                tensor.wait_ge(sv.h, V_de[(x - 1, 15)])
                            if r == 1 and a == 0:
                                tensor.wait_ge(sv.h, V_de[(x, 7)])
                            s = (r * 64 + a) % 16
                            inst = None
                            for j in range(8):
                                inst = tensor.matmul(
                                    out=pD[:, j * TC:(j + 1) * TC],
                                    lhsT=wdq[:, s, j * 128:(j + 1) * 128],
                                    rhs=Yq3[:, a, :],
                                    start=(a == 0 and j % 2 == 0),
                                    stop=(a == NHC - 1 and j % 2 == 1))
                            st.inc(inst)
                            assert st.n == T_dn[(x, r, a)]
                    for g in range(8):
                        half, dcg = divmod(g, 4)
                        tensor.wait_ge(sv.h, V_de[(x, dcg * 4 + 3)])
                        if (x, g) != (0, 0):
                            pg = (x, g - 1) if g else (x - 1, 7)
                            tensor.wait_ge(sv.h, V_as[pg])
                        inst = None
                        for j in range(4):
                            dc = dcg * 4 + j
                            inst = tensor.matmul(
                                out=ptr[:, j * 128:(j + 1) * 128],
                                lhsT=x2f[:, dc * TC + half * 128:
                                         dc * TC + (half + 1) * 128],
                                rhs=K.c_identf, start=(j == 0), stop=(j == 3))
                        st.inc(inst)
                        assert st.n == T_as[(x, g)]

            @blk.vector
            def _(vector):
                def maxes(x):
                    for ag in range(32):
                        vector.wait_ge(ss.h, SS_ev[(x, ag)])
                        ysl = Ybf[:, ag * 512:(ag + 1) * 512]
                        if ag == 0:
                            sv.chain(vector, vector.tensor_scalar_mul(
                                rm2, ysl, -1.0))
                        else:
                            sv.chain(vector, vector.scalar_tensor_tensor(
                                out=rm2, in0=ysl, scalar=-1.0, in1=rm2,
                                op0=ALU.mult, op1=ALU.max))
                        sv.chain(vector, vector.tensor_tensor(
                            out=rm2, in0=rm2, in1=ysl, op=ALU.max))
                    if x >= 1:
                        vector.wait_ge(st.h, T_rmt[x - 1])
                    sv.chain(vector, vector.tensor_tensor(
                        out=rm, in0=rm2[:, 0:TC], in1=rm2[:, TC:2 * TC],
                        op=ALU.max))
                    assert sv.n == V_rmf[x]
                maxes(0)
                for x in range(NTCX):
                    vector.wait_ge(st.h, T_rmt[x])
                    for j in range(2):
                        sv.chain(vector, vector.reduce_max(
                            out=sm[:, j:j + 1],
                            in_=ptr[:, j * 128:(j + 1) * 128], axis=AXX))
                    sv.chain(vector, vector.tensor_scalar_max(sm[:, 2:4],
                                                              sm[:, 0:2],
                                                              Q_EPS))
                    sv.chain(vector, vector.reciprocal(out=sm[:, 4:6],
                                                       in_=sm[:, 2:4]))
                    sv.chain(vector, vector.tensor_scalar_mul(sm[:, 4:6],
                                                              sm[:, 4:6],
                                                              128.0))
                    sv.chain(vector, vector.tensor_scalar(
                        out=sm[:, 6:8], in0=sm[:, 2:4],
                        scalar1=K.tiny[:, 9:10], scalar2=float(2.0 ** -15),
                        op0=ALU.mult, op1=ALU.mult))
                    sv.chain(vector, vector.tensor_scalar_mul(
                        sm[:, 6:8], sm[:, 6:8], K.c_sqrt2))
                    assert sv.n == V_sm[x]
                    vector.wait_ge(st.h, T_smt[x])
                    sv.inc(vector.tensor_copy(out=rw1, in_=pRow))
                    assert sv.n == V_rw[x]
                    vector.wait_ge(st.h, T_k1[x])
                    sv.chain(vector, vector.tensor_copy(out=sbc[:, 0:TC],
                                                        in_=ptr[:, 0:TC]))
                    sv.chain(vector, vector.tensor_copy(out=sbc[:, TC:2 * TC],
                                                        in_=sbc[:, 0:TC]))
                    sv.inc(vector.tensor_copy(out=aYb, in_=ptr[:, TC:2 * TC]))
                    assert sv.n == V_bc[x]
                    for ag in range(32):
                        if x >= 1:
                            vector.wait_ge(st.h, T_dn[(x - 1, 1, 2 * ag + 1)])
                        ysl = Ybf[:, ag * 512:(ag + 1) * 512]
                        sv.chain(vector, vector.tensor_mul(yt, ysl, sbc))
                        sv.chain(vector, vector.tensor_scalar(
                            out=t1, in0=yt, scalar1=C32, scalar2=-C32,
                            op0=ALU.add, op1=ALU.add))
                        sv.chain(vector, vector.tensor_scalar(
                            out=t2, in0=yt, scalar1=C64, scalar2=-C64,
                            op0=ALU.add, op1=ALU.add))
                        sv.chain(vector, vector.scalar_tensor_tensor(
                            out=yt, in0=t1, scalar=K.kk, in1=t2,
                            op0=ALU.mult, op1=ALU.add))
                        sv.chain(vector, vector.tensor_scalar(
                            out=Yq2[:, ag * 512:(ag + 1) * 512], in0=yt,
                            scalar1=-128.0, scalar2=127.0, op0=ALU.max,
                            op1=ALU.min))
                        assert sv.n == V_q[(x, ag)]
                    for dc in range(NDC):
                        j = dc % 8
                        vector.wait_ge(st.h, T_dn[(x, dc // 8, NHC - 1)])
                        vector.wait_ge(ld.h, WOSC[(x, dc)])
                        if x >= 1:
                            vector.wait_ge(st.h, T_as[(x - 1, 4 + dc // 4)])
                        sv.chain(vector, vector.tensor_mul(
                            yt[:, 0:TC], pD[:, j * TC:(j + 1) * TC], aYb))
                        sv.inc(vector.tensor_add(x2f[:, dc * TC:(dc + 1) * TC],
                                                 yt[:, 0:TC], wos_[:, dc, :]))
                        assert sv.n == V_de[(x, dc)]
                    for g in range(8):
                        vector.wait_ge(st.h, T_as[(x, g)])
                        vector.wait_ge(ld.h, X1C[(x, g)])
                        if (x, g) not in ((0, 0), (0, 1)):
                            pg = (x, g - 2) if g >= 2 else (x - 1, g + 6)
                            vector.wait_ge(sout.h, G_out[pg])
                        sv.inc(vector.tensor_add(otk_[:, g % 2, :], ptr,
                                                 x1b_[:, g, :]))
                        assert sv.n == V_as[(x, g)]
                    if x + 1 < NTCX:
                        maxes(x + 1)


PHASES = [phase_init, phase_p0, phase_p1, phase_p2, phase_p3, phase_p4,
          phase_p5, phase_p6, phase_pq, phase_p7, phase_p8a, phase_p8b]


def emit_dev_dumps(K):
    nc = K.nc
    outs = {}
    with nc.Block() as blk:
        @blk.sync
        def _(sync):
            for name in DEV_DUMPS:
                src = getattr(K, name)
                if isinstance(src, Sem):
                    continue
                shape = list(src.shape)
                dt = src.tensor.dtype if hasattr(src, "tensor") else F32
                o = nc.dram_tensor("dbg_" + name, shape, dt, kind="ExternalOutput").ap()
                outs[name] = o
                K.sout.dinc(sync.dma_start(out=o, in_=src))
            sync.wait_ge(K.sout.h, K.sout.n)
    return outs


def build_nc():
    K = KB()
    for ph in PHASES:
        ph(K)
        if DEV_STOP == ph.__name__:
            break
    if DEV_DUMPS:
        emit_dev_dumps(K)
    return K



def _tile4(w_in_out, icn, ocn):
    w = np.ascontiguousarray(w_in_out, dtype=np.float32)
    return np.ascontiguousarray(
        w.reshape(icn, 128, ocn, 128).transpose(2, 1, 0, 3))


def _tile4_inmajor(w_in_out, icn, ocn):
    w = np.ascontiguousarray(w_in_out, dtype=np.float32)
    return np.ascontiguousarray(
        w.reshape(icn, 128, ocn, 128).transpose(0, 1, 2, 3))


def _sylvester(n):
    H = np.array([[1.0]], dtype=np.float64)
    while H.shape[0] < n:
        H = np.block([[H, H], [H, -H]])
    return H


def _chunk_major_vec(v):
    return np.ascontiguousarray(v.reshape(NDC, 128).T, dtype=np.float32)


_SHARED_KEYS = ["cwT", "dwT", "cb", "db", "ldl", "abv", "bbv", "wg0", "wu0",
                "wd0", "wq", "wk", "wv", "wo", "gq", "gk", "wg1", "wu1", "wd1",
                "identb", "identf", "cmask", "h128", "h64b", "onesb"]


def prep_shared(I):
    f32 = np.float32
    S = {}
    S["cwT"] = _tile4(np.asarray(I["L0_attn_C_w"], f32).T, NDC, NDC)
    S["dwT"] = _tile4(np.asarray(I["L0_attn_D_w"], f32).T, NDC, NDC)
    S["cb"] = _chunk_major_vec(np.asarray(I["L0_attn_C_b"], f32))
    S["db"] = _chunk_major_vec(np.asarray(I["L0_attn_D_b"], f32))
    S["ldl"] = _chunk_major_vec(np.asarray(I["L0_attn_log_delta"], f32))
    A = np.asarray(I["L0_attn_A"], f32)
    B = np.asarray(I["L0_attn_B"], f32)
    S["abv"] = np.ascontiguousarray(
        A.reshape(NDC, 128, 64).transpose(1, 0, 2).reshape(128, NDC * 64))
    S["bbv"] = np.ascontiguousarray(
        B.reshape(NDC, 128, 64).transpose(1, 0, 2).reshape(128, NDC * 64))
    S["wg0"] = _tile4(np.asarray(I["L0_ffn_w_gate"], f32).T, NDC, NHC)
    S["wu0"] = _tile4(np.asarray(I["L0_ffn_w_up"], f32).T, NDC, NHC)
    S["wd0"] = np.ascontiguousarray(np.asarray(I["L0_ffn_w_down"], f32).T.reshape(NHC, 128, NDC, 128))
    S["wq"] = _tile4(np.asarray(I["L1_attn_Wq"], f32), NDC, NH)
    S["wk"] = _tile4(np.asarray(I["L1_attn_Wk"], f32), NDC, NKV)
    S["wv"] = _tile4(np.asarray(I["L1_attn_Wv"], f32), NDC, NKV)
    S["wo"] = _tile4(np.asarray(I["L1_attn_Wo"], f32), NH, NDC)
    S["gq"] = np.ascontiguousarray(np.asarray(I["L1_attn_gq"], f32).reshape(128, 1))
    S["gk"] = np.ascontiguousarray(np.asarray(I["L1_attn_gk"], f32).reshape(128, 1))
    S["wg1"] = _tile4(np.asarray(I["L1_ffn_w_gate"], f32).T, NDC, NHC)
    S["wu1"] = _tile4(np.asarray(I["L1_ffn_w_up"], f32).T, NDC, NHC)
    wdT = np.asarray(I["L1_ffn_w_down"], f32).T
    p = np.arange(128)
    rows = (p % 64)[None, :] * 128 + 2 * np.arange(64)[:, None] + (p // 64)[None, :]
    wd_perm = wdT[rows]
    S["wd1"] = np.ascontiguousarray(wd_perm.reshape(NHC, 128, NDC, 128))
    S["identb"] = np.eye(128, dtype=np.float32).astype(BF)
    S["identf"] = np.eye(128, dtype=np.float32)
    S["cmask"] = np.ascontiguousarray(
        np.triu(np.full((128, 128), -1e30, np.float32), 1))
    S["h128"] = _sylvester(128).astype(np.float32).astype(BF)
    h64 = _sylvester(64)
    z = np.zeros((64, 64))
    S["h64b"] = np.block([[h64, z], [z, h64]]).astype(np.float32).astype(BF)
    S["onesb"] = np.ones((128, 1), np.float32).astype(BF)
    S["onesf"] = np.ones((128, 1), np.float32)
    S["onesr"] = np.ones((1, 128), np.float32)
    return S


def kernel(**inputs):
    S = prep_shared(inputs)
    K = build_nc()
    in_maps = []
    for b in range(8):
        m = dict(S)
        m["x0"] = np.ascontiguousarray(np.asarray(inputs["L0_attn_x"][b], np.float32))
        in_maps.append(m)
    res = run_bass_kernel_spmd(K.nc, in_maps, core_ids=list(range(8)))
    out = np.stack([np.asarray(res.results[b]["out"]) for b in range(8)], axis=0)
    return out
